# version 01
# baseline (speedup 1.0000x reference)
import numpy as np
import concourse.bass as bass
import concourse.mybir as mybir
from concourse.bass_utils import run_bass_kernel_spmd

F32 = mybir.dt.float32
BF16 = mybir.dt.bfloat16
I32 = mybir.dt.int32
AF = mybir.ActivationFunctionType
ALU = mybir.AluOpType

D, DC, S, SF, H, HD, NKT = 1024, 8, 2064, 4096, 16, 64, 32
EPS = 1e-5
N_SUB = 4
DEV_OUT = None

QCH = [(0, 512), (512, 512), (1024, 512), (1536, 512), (2048, 16)]
FQC = [(i * 344, 344) for i in range(6)]


def _sperm():
    s = np.arange(S)
    return (16 * (s % 129) + (15 - s // 129)).astype(np.int64)


def _rd_pieces(c0, wdt):
    outp = []
    s = c0
    while s < c0 + wdt:
        rd, t0 = divmod(s, 129)
        n = min(129 - t0, c0 + wdt - s)
        outp.append((s - c0, rd, t0, n))
        s += n
    return outp


class B:

    def __init__(self):
        self.nc = bass.Bass("TRN2", target_bir_lowering=False, debug=False)
        self._ctx = []
        self.cnt = {}
        self.step = {}
        self.sems = {}
        for n in ("TE", "VE", "SE", "GE"):
            self._sem(n, 1)
        for n in ("LD", "ST", "GD"):
            self._sem(n, 16)

    def _sem(self, n, st):
        cm = self.nc.semaphore(n)
        self.sems[n] = cm.__enter__()
        self._ctx.append(cm)
        self.cnt[n], self.step[n] = 0, st

    def sbt(self, name, shape, dt=F32):
        cm = self.nc.sbuf_tensor(name, shape, dt)
        self._ctx.append(cm)
        return cm.__enter__()

    def pst(self, name, shape):
        cm = self.nc.psum_tensor(name, shape, F32)
        self._ctx.append(cm)
        return cm.__enter__()

    def inc(self, inst, name):
        self.cnt[name] += self.step[name]
        inst.then_inc(self.sems[name], self.step[name])
        return self.cnt[name]

    def w(self, eng, name, val=None):
        eng.wait_ge(self.sems[name], self.cnt[name] if val is None else val)

    def barrier(self):
        marks = dict(self.cnt)
        ALL = ("TE", "VE", "SE", "GE", "LD", "ST", "GD")
        own = {"tensor": "TE", "vector": "VE", "scalar": "SE", "gpsimd": "GE",
               "sync": None}
        with self.nc.Block() as blk:
            for engname, skip in own.items():
                def mk(skip):
                    def f(eng):
                        for s_ in ALL:
                            if s_ != skip:
                                eng.wait_ge(self.sems[s_], marks[s_])
                    return f
                getattr(blk, engname)(mk(skip))


def build_program():
    b = B()
    nc = b.nc
    inc, w, barrier = b.inc, b.w, b.barrier

    def din(name, shape, dt=F32):
        return nc.dram_tensor(name, shape, dt, kind="ExternalInput").ap()

    xe = din("xe", [D, S])
    xf = din("xf", [D, SF])
    wsA = din("wsA", [24, 128, DC * 128])
    b_in = din("b_in", [3072, 1])
    wsOA = din("wsOA", [8, 128, DC * 128])
    b_oa = din("b_oa", [D, 1])
    w1t = din("w1t", [D, 4096])
    b1 = din("b1", [4096, 1])
    w2t = din("w2t", [4096, D])
    b2 = din("b2", [D, 1])
    wsQB = din("wsQB", [24, 128, DC * 128])
    b_qb = din("b_qb", [3072, 1])
    wsOB = din("wsOB", [8, 128, DC * 128])
    b_ob = din("b_ob", [D, 1])
    btab = din("btab", [128, 8072])
    mkc = din("mkc", [16, 128, 144])
    mks = din("mks", [16, 24, 144])
    idin = din("idin", [128, 256])
    out = nc.dram_tensor("out", [D, S], F32, kind="ExternalOutput").ap()
    out2 = (nc.dram_tensor("out2", [3 * D, SF], F32,
                           kind="ExternalOutput").ap()
            if DEV_OUT is not None else None)

    def scr(name, shape, dt=F32):
        return nc.dram_tensor(name, shape, dt, kind="Internal").ap()

    QS = scr("QS", [D, S])
    KS = scr("KS", [D, SF])
    VS = scr("VS", [D, SF])
    CTXS = scr("CTXS", [D, S])
    QKB = scr("QKB", [3 * D, S])
    HQs = scr("HQs", [4096, S], BF16)
    W1Q = scr("W1Q", [D, 4096], BF16)
    W2Q = scr("W2Q", [4096, D], BF16)
    DENS = scr("DENS", [H, S])
    CTXB = scr("CTXB", [D, S])

    X = b.sbt("X", [128, DC * S])
    A1 = b.sbt("A1", [128, 14464])
    A2 = b.sbt("A2", [128, 4224])
    A3 = b.sbt("A3", [128, 4224])
    XQ = b.sbt("XQ", [128, DC * 1032], BF16)
    HQ16 = b.sbt("HQ16", [128, 1032], BF16)
    WQ16 = b.sbt("WQ16", [128, 2048], BF16)
    IT = b.sbt("IT", [128, 256])
    QI = b.sbt("QI", [128, 1056], I32)
    QI2 = b.sbt("QI2", [128, 1056], I32)
    SM = b.sbt("SM", [1, 512])
    SRW = b.sbt("SRW", [1, 512])
    SB2 = b.sbt("SB2", [1, 512])
    ST2 = b.sbt("ST2", [1, 512])
    DRW = b.sbt("DRW", [1, 2112])
    CST = b.sbt("CST", [128, 8])
    ONR = b.sbt("ONR", [1, 128])
    ON2 = b.sbt("ON2", [128, 130])
    BCL = b.sbt("BCL", [128, 48])
    BQC = b.sbt("BQC", [128, 48])

    PB0 = b.pst("PB0", [128, 1024])
    PB1 = b.pst("PB1", [128, 1024])
    PC = [b.pst(f"PC{i}", [128, 512]) for i in range(4)]
    sems = b.sems

    with nc.Block() as blk:
        @blk.sync
        def _(sy):
            for c in range(DC):
                inc(sy.dma_start(out=X[:, c * S:(c + 1) * S],
                                 in_=xe[128 * c:128 * (c + 1), :]), "LD")
            inc(sy.dma_start(out=IT[:], in_=idin), "LD")

        @blk.vector
        def _(ve):
            ve.memset(CST[:, 0:1], 0.125)
            ve.memset(CST[:, 1:2], -8.0)
            ve.memset(CST[:, 2:3], EPS)
            ve.memset(CST[:, 3:4], 1.0 / 1024.0)
            ve.memset(CST[:, 4:5], 1.0)
            ve.memset(ONR[:], 1.0)
            inc(ve.memset(ON2[:], 1.0), "VE")
    barrier()

    def ln_group(src_fn, wd):
        te0 = b.cnt["TE"]
        with nc.Block() as blk:
            @blk.scalar
            def _(se):
                for c in range(DC):
                    if c >= 2:
                        se.wait_ge(sems["TE"], te0 + c - 1)
                    inc(se.activation(out=A2[:, (c % 2) * 512:(c % 2) * 512 + wd],
                                      in_=src_fn(c), func=AF.Square), "SE")

            @blk.tensor
            def _(te):
                se0 = b.cnt["SE"] - DC
                for c in range(DC):
                    te.wait_ge(sems["SE"], se0 + c + 1)
                    te.matmul(PB0[0:1, 0:wd], CST[:, 4:5], src_fn(c),
                              start=(c == 0), stop=(c == DC - 1))
                    inc(te.matmul(PB0[0:1, 512:512 + wd], CST[:, 4:5],
                                  A2[:, (c % 2) * 512:(c % 2) * 512 + wd],
                                  start=(c == 0), stop=(c == DC - 1)), "TE")

            @blk.vector
            def _(ve):
                w(ve, "TE")
                m_, b_, t_, r_ = (SM[:, 0:wd], SB2[:, 0:wd], ST2[:, 0:wd],
                                  SRW[:, 0:wd])
                inc(ve.tensor_scalar(out=m_, in0=PB0[0:1, 0:wd],
                                     scalar1=1.0 / 1024.0, scalar2=None,
                                     op0=ALU.mult), "VE")
                inc(ve.tensor_scalar(out=b_, in0=PB0[0:1, 512:512 + wd],
                                     scalar1=1.0 / 1024.0, scalar2=None,
                                     op0=ALU.mult), "VE")
                w(ve, "VE")
                inc(ve.tensor_mul(out=t_, in0=m_, in1=m_), "VE")
                w(ve, "VE")
                inc(ve.tensor_sub(out=b_, in0=b_, in1=t_), "VE")
                w(ve, "VE")
                inc(ve.tensor_scalar(out=b_, in0=b_, scalar1=EPS, scalar2=None,
                                     op0=ALU.add), "VE")

            @blk.scalar
            def _(se):
                w(se, "VE")
                inc(se.activation(out=ST2[:, 0:wd], in_=SB2[:, 0:wd],
                                  func=AF.Sqrt), "SE")

            @blk.vector
            def _(ve):
                w(ve, "SE")
                b_, t_, r_ = SB2[:, 0:wd], ST2[:, 0:wd], SRW[:, 0:wd]
                inc(ve.reciprocal(out=r_, in_=t_), "VE")
                w(ve, "VE")
                inc(ve.tensor_mul(out=t_, in0=r_, in1=r_), "VE")
                w(ve, "VE")
                inc(ve.tensor_mul(out=t_, in0=t_, in1=b_), "VE")
                w(ve, "VE")
                inc(ve.tensor_scalar(out=t_, in0=t_, scalar1=-0.5, scalar2=1.5,
                                     op0=ALU.mult, op1=ALU.add), "VE")
                w(ve, "VE")
                inc(ve.tensor_mul(out=r_, in0=r_, in1=t_), "VE")
        barrier()

    def normalize_tile(wd, src_ap, dst_ap):
        with nc.Block() as blk:
            @blk.tensor
            def _(te):
                w(te, "VE")
                te.matmul(PB1[:, 0:wd], ON2[0:1, 0:128], SM[:, 0:wd],
                          start=True, stop=True)
                inc(te.matmul(PB1[:, 512:512 + wd], ON2[0:1, 0:128],
                              SRW[:, 0:wd], start=True, stop=True), "TE")

            @blk.vector
            def _(ve):
                w(ve, "TE")
                inc(ve.tensor_sub(out=dst_ap, in0=src_ap, in1=PB1[:, 0:wd]),
                    "VE")
                w(ve, "VE")
                inc(ve.tensor_mul(out=dst_ap, in0=dst_ap,
                                  in1=PB1[:, 512:512 + wd]), "VE")

    def ladder(ve, src, dst, halfscale, wd, lo, hi):
        s_ = A2[:, 0:wd]
        c1 = A2[:, 1056:1056 + wd]
        c2 = A2[:, 2112:2112 + wd]
        qf = A2[:, 3168:3168 + wd]
        qi = QI[:, 0:wd]
        inc(ve.tensor_scalar(out=qi, in0=src, scalar1=float(halfscale),
                             scalar2=None, op0=ALU.mult), "VE")
        w(ve, "VE")
        inc(ve.tensor_copy(out=qf, in_=qi), "VE")
        w(ve, "VE")
        inc(ve.scalar_tensor_tensor(out=s_, in0=src, scalar=float(halfscale),
                                    in1=qf, op0=ALU.mult, op1=ALU.subtract),
            "VE")
        w(ve, "VE")
        inc(ve.tensor_scalar(out=c1, in0=s_, scalar1=0.25, scalar2=None,
                             op0=ALU.is_gt), "VE")
        inc(ve.tensor_scalar(out=c2, in0=s_, scalar1=-0.25, scalar2=None,
                             op0=ALU.is_lt), "VE")
        w(ve, "VE")
        inc(ve.scalar_tensor_tensor(out=c1, in0=s_, scalar=0.75, in1=c1,
                                    op0=ALU.is_ge, op1=ALU.add), "VE")
        inc(ve.scalar_tensor_tensor(out=c2, in0=s_, scalar=-0.75, in1=c2,
                                    op0=ALU.is_le, op1=ALU.add), "VE")
        w(ve, "VE")
        inc(ve.scalar_tensor_tensor(out=c1, in0=qf, scalar=2.0, in1=c1,
                                    op0=ALU.mult, op1=ALU.add), "VE")
        w(ve, "VE")
        inc(ve.tensor_sub(out=c1, in0=c1, in1=c2), "VE")
        w(ve, "VE")
        inc(ve.tensor_scalar(out=dst, in0=c1, scalar1=float(hi),
                             scalar2=float(lo), op0=ALU.min, op1=ALU.max),
            "VE")

    def intq(ve, pre_fn, dst, wd, lo, hi):
        qi = QI[:, 0:wd]
        q2 = QI2[:, 0:wd]
        inc(pre_fn(qi), "VE")
        w(ve, "VE")
        inc(ve.tensor_scalar(out=q2, in0=qi, scalar1=6, scalar2=None,
                             op0=ALU.arith_shift_right), "VE")
        w(ve, "VE")
        inc(ve.tensor_scalar(out=q2, in0=q2, scalar1=1, scalar2=None,
                             op0=ALU.bitwise_and), "VE")
        w(ve, "VE")
        inc(ve.tensor_tensor(out=qi, in0=qi, in1=q2, op=ALU.add), "VE")
        w(ve, "VE")
        inc(ve.tensor_scalar(out=qi, in0=qi, scalar1=31, scalar2=None,
                             op0=ALU.add), "VE")
        w(ve, "VE")
        inc(ve.tensor_scalar(out=qi, in0=qi, scalar1=6, scalar2=None,
                             op0=ALU.arith_shift_right), "VE")
        w(ve, "VE")
        inc(ve.tensor_scalar(out=dst, in0=qi, scalar1=float(hi),
                             scalar2=float(lo), op0=ALU.min, op1=ALU.max),
            "VE")

    def load_bias_cols(bsrc, blk0, n):
        with nc.Block() as blk:
            @blk.sync
            def _(sy):
                w(sy, "VE")
                src = bass.AP(bsrc.tensor, 128 * blk0, [[1, 128], [128, n]])
                inc(sy.dma_start(out=BCL[:, 0:n], in_=src,
                                 allow_slow_non_contiguous=True), "LD")
        barrier()

    def gemm_fp32(lhs_list, rhs_fn, o, wd, evac_fn, rhs16=False):
        nblk = len(lhs_list)
        for bb in range(nblk):
            buf = (bb % 2) * 1024
            with nc.Block() as blk:
                @blk.sync
                def _(sy, bb=bb, buf=buf):
                    if bb >= 2:
                        w(sy, "TE")
                    dst = (WQ16 if rhs16 else A1)[:, buf:buf + 1024]
                    inc(sy.dma_start(out=dst, in_=lhs_list[bb]), "LD")

                @blk.tensor
                def _(te, bb=bb, buf=buf):
                    w(te, "LD")
                    if bb >= 2:
                        w(te, "VE")
                    ps = PB0[:, (bb % 2) * 512:(bb % 2) * 512 + wd]
                    lt = (WQ16 if rhs16 else A1)
                    for c in range(DC):
                        i_ = te.matmul(ps, lt[:, buf + 128 * c:buf + 128 * (c + 1)],
                                       rhs_fn(c), start=(c == 0),
                                       stop=(c == DC - 1))
                    inc(i_, "TE")
            evac_fn(bb, o, wd)

    def evac_spill(dst_row_fn, withbias=True):
        def f(bb, o, wd):
            stg = A1[:, 2048 + (bb % 2) * 512:2048 + (bb % 2) * 512 + wd]
            with nc.Block() as blk:
                @blk.vector
                def _(ve):
                    w(ve, "TE")
                    if bb >= 2:
                        w(ve, "ST")
                    ps = PB0[:, (bb % 2) * 512:(bb % 2) * 512 + wd]
                    if withbias:
                        inc(ve.tensor_scalar(out=stg, in0=ps,
                                             scalar1=BCL[:, bb:bb + 1],
                                             scalar2=None, op0=ALU.add), "VE")
                    else:
                        inc(ve.tensor_copy(out=stg, in_=ps), "VE")

                @blk.scalar
                def _(se):
                    w(se, "VE")
                    inc(se.dma_start(out=dst_row_fn(bb)[:, o:o + wd], in_=stg),
                        "ST")
        return f

    def evac_residual():
        def f(bb, o, wd):
            with nc.Block() as blk:
                @blk.vector
                def _(ve):
                    w(ve, "TE")
                    ps = PB0[:, (bb % 2) * 512:(bb % 2) * 512 + wd]
                    xs = X[:, bb * S + o:bb * S + o + wd]
                    inc(ve.scalar_tensor_tensor(out=xs, in0=ps,
                                                scalar=BCL[:, bb:bb + 1],
                                                in1=xs, op0=ALU.add,
                                                op1=ALU.add), "VE")
        return f

    load_bias_cols(b_in, 0, 8)
    for (o, wd) in QCH:
        ln_group(lambda c: X[:, c * S + o:c * S + o + wd], wd)
        for c in range(DC):
            normalize_tile(wd, X[:, c * S + o:c * S + o + wd],
                           A1[:, 4096 + 512 * c:4096 + 512 * c + wd])
        barrier()
        gemm_fp32([wsA[bb] for bb in range(8)],
                  lambda c: A1[:, 4096 + 512 * c:4096 + 512 * c + wd],
                  o, wd,
                  evac_spill(lambda bb: QS[128 * bb:128 * (bb + 1), :]))
        barrier()

    load_bias_cols(b_in, 8, 16)

    def kv_dst(bb):
        return (KS if bb < 8 else VS)[128 * (bb % 8):128 * (bb % 8 + 1), :]

    for (o, wd) in [(i * 512, 512) for i in range(8)]:
        with nc.Block() as blk:
            @blk.sync
            def _(sy, o=o, wd=wd):
                w(sy, "TE")
                w(sy, "SE")
                for c in range(DC):
                    inc(sy.dma_start(out=A1[:, 4096 + 512 * c:4608 + 512 * c],
                                     in_=xf[128 * c:128 * (c + 1), o:o + wd]),
                        "LD")
        barrier()
        ln_group(lambda c: A1[:, 4096 + 512 * c:4096 + 512 * c + wd], wd)
        for c in range(DC):
            normalize_tile(wd, A1[:, 4096 + 512 * c:4096 + 512 * c + wd],
                           A1[:, 4096 + 512 * c:4096 + 512 * c + wd])
        barrier()
        gemm_fp32([wsA[bb] for bb in range(8, 24)],
                  lambda c: A1[:, 4096 + 512 * c:4096 + 512 * c + wd],
                  o, wd, evac_spill(kv_dst))
        barrier()

    if N_SUB >= 1 and DEV_OUT != "qkv":
        for p_ in range(8):
            with nc.Block() as blk:
                @blk.sync
                def _(sy, p_=p_):
                    w(sy, "TE")
                    w(sy, "VE")
                    w(sy, "SE")
                    inc(sy.dma_start(out=A1[:, 0:4096],
                                     in_=KS[128 * p_:128 * (p_ + 1), :]), "LD")
                    inc(sy.dma_start(out=A1[:, 4096:8192],
                                     in_=VS[128 * p_:128 * (p_ + 1), :]), "LD")
                    inc(sy.dma_start(out=A1[:, 8192:10256],
                                     in_=QS[128 * p_:128 * (p_ + 1), :]), "LD")
            barrier()
            for hl in range(2):
                with nc.Block() as blk:
                    @blk.tensor
                    def _(te, hl=hl):
                        ve0 = b.cnt["VE"]
                        for kt in range(NKT):
                            if kt >= 2:
                                te.wait_ge(sems["VE"], ve0 + kt - 1)
                            inc(te.transpose(
                                PC[kt % 2][0:128, 0:64],
                                A1[64 * hl:64 * hl + 64,
                                   4096 + 128 * kt:4224 + 128 * kt],
                                IT[64 * hl:64 * hl + 64, 0:64]), "TE")

                    @blk.vector
                    def _(ve, hl=hl):
                        te0 = b.cnt["TE"] - NKT
                        for kt in range(NKT):
                            ve.wait_ge(sems["TE"], te0 + kt + 1)
                            inc(ve.tensor_copy(
                                out=A3[:, 2080 * hl + 65 * kt:
                                       2080 * hl + 65 * kt + 64],
                                in_=PC[kt % 2][0:128, 0:64]), "VE")
                        inc(ve.memset(bass.AP(A3, 2080 * hl + 64,
                                              [[4224, 128], [65, 32]]), 1.0),
                            "VE")
            barrier()
            for hl in range(2):
                h_ = 2 * p_ + hl
                with nc.Block() as blk:
                    @blk.sync
                    def _(sy, h_=h_):
                        w(sy, "TE")
                        i0 = 1 + 256 * h_
                        inc(sy.dma_start(out=A2[:, 0:4217],
                                         in_=btab[:, i0:i0 + 4217]), "LD")
                barrier()
                for (qo, qw) in QCH:
                    scm = [0] * NKT
                    avm = [0] * NKT
                    se_pre = b.cnt["SE"]
                    exm = [se_pre + kt + 1 for kt in range(NKT)]
                    with nc.Block() as blk:
                        @blk.tensor
                        def _(te, hl=hl, qo=qo, qw=qw):
                            for kt in range(NKT):
                                slot = kt % 4
                                pt = (PB0, PB0, PB1, PB1)[slot]
                                ph = (slot % 2) * 512
                                if kt == 0:
                                    w(te, "VE")
                                    w(te, "SE")
                                    w(te, "LD")
                                if kt >= 4:
                                    te.wait_ge(sems["SE"], exm[kt - 4])
                                te.matmul(pt[0:128, ph:ph + qw],
                                          A1[64 * hl:64 * hl + 64,
                                             128 * kt:128 * (kt + 1)],
                                          A1[64 * hl:64 * hl + 64,
                                             8192 + qo:8192 + qo + qw],
                                          start=True, stop=False)
                                pcs = _rd_pieces(qo, qw)
                                for pi, (soff, rd, t0, n_) in enumerate(pcs):
                                    i0 = 248 - 8 * kt + 256 * rd + t0
                                    i_ = te.matmul(
                                        pt[0:128, ph + soff:ph + soff + n_],
                                        IT[:, 64:192], A2[:, i0:i0 + n_],
                                        start=False, stop=(pi == len(pcs) - 1))
                                scm[kt] = inc(i_, "TE")
                                if kt >= 4:
                                    ktp = kt - 4
                                    te.wait_ge(sems["SE"], exm[ktp])
                                    avm[ktp] = inc(te.matmul(
                                        PC[0][0:65, 0:qw],
                                        A3[:, 2080 * hl + 65 * ktp:
                                           2080 * hl + 65 * ktp + 65],
                                        A1[:, 10256 + 512 * (ktp % 4):
                                           10256 + 512 * (ktp % 4) + qw],
                                        start=(ktp == 0), stop=False), "TE")
                            for ktp in range(NKT - 4, NKT):
                                te.wait_ge(sems["SE"], exm[ktp])
                                avm[ktp] = inc(te.matmul(
                                    PC[0][0:65, 0:qw],
                                    A3[:, 2080 * hl + 65 * ktp:
                                       2080 * hl + 65 * ktp + 65],
                                    A1[:, 10256 + 512 * (ktp % 4):
                                       10256 + 512 * (ktp % 4) + qw],
                                    start=(ktp == 0), stop=(ktp == NKT - 1)),
                                    "TE")

                        @blk.scalar
                        def _(se, qw=qw):
                            for kt in range(NKT):
                                slot = kt % 4
                                pt = (PB0, PB0, PB1, PB1)[slot]
                                ph = (slot % 2) * 512
                                se.wait_ge(sems["TE"], scm[kt])
                                if kt >= 4:
                                    se.wait_ge(sems["TE"], avm[kt - 4])
                                exm[kt] = inc(se.activation(
                                    out=A1[:, 10256 + 512 * slot:
                                           10256 + 512 * slot + qw],
                                    in_=pt[0:128, ph:ph + qw], func=AF.Exp,
                                    bias=CST[:, 1:2], scale=CST[:, 0:1]),
                                    "SE")
                    with nc.Block() as blk:
                        @blk.vector
                        def _(ve, qw=qw):
                            w(ve, "TE")
                            inc(ve.reciprocal(out=A1[64:65, 13328:13328 + qw],
                                              in_=PC[0][64:65, 0:qw]), "VE")

                        @blk.tensor
                        def _(te, qw=qw):
                            w(te, "VE")
                            inc(te.matmul(PB1[0:64, 0:qw], ON2[64:65, 0:64],
                                          A1[64:65, 13328:13328 + qw],
                                          start=True, stop=True), "TE")

                        @blk.vector
                        def _(ve, qw=qw):
                            w(ve, "TE")
                            inc(ve.tensor_copy(out=A1[0:64, 13840:13840 + qw],
                                               in_=PB1[0:64, 0:qw]), "VE")
                            w(ve, "VE")
                            inc(ve.tensor_mul(out=A1[0:64, 12304:12304 + qw],
                                              in0=PC[0][0:64, 0:qw],
                                              in1=A1[0:64, 13840:13840 + qw]),
                                "VE")

                        @blk.scalar
                        def _(se, h_=h_, qo=qo, qw=qw):
                            w(se, "VE")
                            inc(se.dma_start(
                                out=CTXS[64 * h_:64 * h_ + 64, qo:qo + qw],
                                in_=A1[0:64, 12304:12304 + qw]), "ST")
                    barrier()

        load_bias_cols(b_oa, 0, 8)
        for (o, wd) in QCH:
            with nc.Block() as blk:
                @blk.sync
                def _(sy, o=o, wd=wd):
                    w(sy, "TE")
                    for c in range(DC):
                        inc(sy.dma_start(
                            out=A1[:, 4096 + 512 * c:4096 + 512 * c + wd],
                            in_=CTXS[128 * c:128 * (c + 1), o:o + wd]), "LD")
            barrier()
            gemm_fp32([wsOA[bb] for bb in range(8)],
                      lambda c: A1[:, 4096 + 512 * c:4096 + 512 * c + wd],
                      o, wd, evac_residual())
            barrier()

    FH = [(0, 512), (512, 512), (1024, 8)]
    FH2 = [(1032, 512), (1544, 512), (2056, 8)]

    def quant_weights():
        with nc.Block() as blk:
            @blk.sync
            def _(sy):
                w(sy, "TE")
                w(sy, "VE")
                w(sy, "SE")
                srcb1 = bass.AP(b1.tensor, 0, [[1, 128], [128, 32]])
                inc(sy.dma_start(out=BCL[:, 0:32], in_=srcb1,
                                 allow_slow_non_contiguous=True), "LD")
                srcb2 = bass.AP(b2.tensor, 0, [[1, 128], [128, 8]])
                inc(sy.dma_start(out=BCL[:, 32:40], in_=srcb2,
                                 allow_slow_non_contiguous=True), "LD")
        barrier()
        with nc.Block() as blk:
            @blk.vector
            def _(ve):
                ladder(ve, BCL[:, 0:32], A1[:, 0:32], 32.0, 32, -128, 127)
                w(ve, "VE")
                inc(ve.tensor_scalar(out=BQC[:, 0:32], in0=A1[:, 0:32],
                                     scalar1=16.0, scalar2=None,
                                     op0=ALU.mult), "VE")
                ladder(ve, BCL[:, 32:40], A1[:, 32:40], 32.0, 8, -128, 127)
                w(ve, "VE")
                inc(ve.tensor_scalar(out=BQC[:, 32:40], in0=A1[:, 32:40],
                                     scalar1=1.0 / 64.0, scalar2=None,
                                     op0=ALU.mult), "VE")
        barrier()
        for c in range(DC):
            with nc.Block() as blk:
                @blk.sync
                def _(sy, c=c):
                    w(sy, "VE")
                    w(sy, "ST")
                    inc(sy.dma_start(out=A1[:, 0:4096],
                                     in_=w1t[128 * c:128 * (c + 1), :]), "LD")
            barrier()
            for qq in range(4):
                with nc.Block() as blk:
                    @blk.vector
                    def _(ve, c=c, qq=qq):
                        if qq > 0:
                            w(ve, "ST")
                        ladder(ve, A1[:, 1024 * qq:1024 * (qq + 1)],
                               XQ[:, 0:1024], 32.0, 1024, -128, 127)

                    @blk.scalar
                    def _(se, c=c, qq=qq):
                        w(se, "VE")
                        inc(se.dma_start(
                            out=W1Q[128 * c:128 * (c + 1),
                                    1024 * qq:1024 * (qq + 1)],
                            in_=XQ[:, 0:1024]), "ST")
            barrier()
        for c in range(32):
            with nc.Block() as blk:
                @blk.sync
                def _(sy, c=c):
                    w(sy, "VE")
                    w(sy, "ST")
                    inc(sy.dma_start(out=A1[:, 0:1024],
                                     in_=w2t[128 * c:128 * (c + 1), :]), "LD")
            barrier()
            with nc.Block() as blk:
                @blk.vector
                def _(ve, c=c):
                    ladder(ve, A1[:, 0:1024], XQ[:, 0:1024], 32.0, 1024,
                           -128, 127)

                @blk.scalar
                def _(se, c=c):
                    w(se, "VE")
                    inc(se.dma_start(out=W2Q[128 * c:128 * (c + 1), :],
                                     in_=XQ[:, 0:1024]), "ST")
            barrier()

    def emit_ffn():
        for th in range(2):
            groups = FH if th == 0 else FH2
            for (o, wd) in groups:
                ln_group(lambda c: X[:, c * S + o:c * S + o + wd], wd)
                for c in range(DC):
                    normalize_tile(wd, X[:, c * S + o:c * S + o + wd],
                                   A1[:, 512 * c:512 * c + wd])
                barrier()
                with nc.Block() as blk:
                    @blk.vector
                    def _(ve, o=o, wd=wd):
                        for c in range(DC):
                            ladder(ve, A1[:, 512 * c:512 * c + wd],
                                   XQ[:, 1032 * c + o - th * 1032:
                                      1032 * c + o - th * 1032 + wd],
                                   8.0, wd, -128, 127)
                barrier()
            for bb in range(32):
                buf = (bb % 2) * 1024
                with nc.Block() as blk:
                    @blk.sync
                    def _(sy, bb=bb, buf=buf):
                        if bb >= 2:
                            w(sy, "TE")
                        srcw = bass.AP(W1Q.tensor, 128 * bb,
                                       [[4096, 128], [128 * 4096, 8],
                                        [1, 128]])
                        inc(sy.dma_start(out=WQ16[:, buf:buf + 1024],
                                         in_=srcw), "LD")

                    @blk.tensor
                    def _(te, bb=bb, buf=buf):
                        w(te, "LD")
                        if bb >= 1:
                            w(te, "VE")
                        for (fo, fw) in [(0, 344), (344, 344), (688, 344)]:
                            ps = PB0[:, 0:fw] if fo == 0 else (
                                PB0[:, 512:512 + fw] if fo == 344 else
                                PB1[:, 0:fw])
                            for c in range(DC):
                                i_ = te.matmul(
                                    ps, WQ16[:, buf + 128 * c:
                                             buf + 128 * (c + 1)],
                                    XQ[:, 1032 * c + fo:1032 * c + fo + fw],
                                    start=(c == 0), stop=(c == DC - 1))
                            inc(i_, "TE")

                    @blk.vector
                    def _(ve, bb=bb):
                        if bb >= 1:
                            w(ve, "ST")
                        te0 = b.cnt["TE"] - 3
                        for gi, (fo, fw) in enumerate(
                                [(0, 344), (344, 344), (688, 344)]):
                            ps = PB0[:, 0:fw] if fo == 0 else (
                                PB0[:, 512:512 + fw] if fo == 344 else
                                PB1[:, 0:fw])
                            ve.wait_ge(sems["TE"], te0 + gi + 1)

                            def pre(qi_ap, ps=ps, bb=bb):
                                return ve.tensor_scalar(
                                    out=qi_ap, in0=ps,
                                    scalar1=BQC[:, bb:bb + 1], scalar2=None,
                                    op0=ALU.add)
                            intq(ve, pre, HQ16[:, fo:fo + fw], fw, 0, 127)

                    @blk.scalar
                    def _(se, bb=bb, th=th):
                        w(se, "VE")
                        inc(se.dma_start(
                            out=HQs[128 * bb:128 * (bb + 1),
                                    1032 * th:1032 * (th + 1)],
                            in_=HQ16[:, 0:1032]), "ST")
            barrier()
            P8 = [PB0[:, 0:344], PB0[:, 512:856], PB1[:, 0:344],
                  PB1[:, 512:856], PC[0][:, 0:344], PC[1][:, 0:344],
                  PC[2][:, 0:344], PC[3][:, 0:344]]
            for (fo, fw) in [(0, 344), (344, 344), (688, 344)]:
                for ic in range(32):
                    hbuf = (ic % 2) * 344
                    wbuf = (ic % 2) * 1024
                    with nc.Block() as blk:
                        @blk.sync
                        def _(sy, ic=ic, fo=fo, fw=fw, hbuf=hbuf, wbuf=wbuf,
                              th=th):
                            if ic >= 2:
                                w(sy, "TE")
                            inc(sy.dma_start(
                                out=HQ16[:, hbuf:hbuf + fw],
                                in_=HQs[128 * ic:128 * (ic + 1),
                                        1032 * th + fo:1032 * th + fo + fw]),
                                "LD")
                            inc(sy.dma_start(
                                out=WQ16[:, wbuf:wbuf + 1024],
                                in_=W2Q[128 * ic:128 * (ic + 1), :]), "LD")

                        @blk.tensor
                        def _(te, ic=ic, fw=fw, hbuf=hbuf, wbuf=wbuf):
                            w(te, "LD")
                            if ic == 0:
                                w(te, "VE")
                            for bb in range(8):
                                i_ = te.matmul(
                                    P8[bb], WQ16[:, wbuf + 128 * bb:
                                                 wbuf + 128 * (bb + 1)],
                                    HQ16[:, hbuf:hbuf + fw],
                                    start=(ic == 0), stop=(ic == 31))
                            inc(i_, "TE")
                with nc.Block() as blk:
                    @blk.vector
                    def _(ve, fo=fo, fw=fw, th=th):
                        w(ve, "TE")
                        for bb in range(8):
                            xo = bb * S + th * 1032 + fo
                            inc(ve.tensor_scalar(
                                out=A1[:, 512 * bb:512 * bb + fw],
                                in0=P8[bb], scalar1=2.0 ** -10,
                                scalar2=BQC[:, 32 + bb:33 + bb],
                                op0=ALU.mult, op1=ALU.add), "VE")
                            ve.wait_ge(sems["VE"], b.cnt["VE"])
                            inc(ve.tensor_add(
                                out=X[:, xo:xo + fw], in0=X[:, xo:xo + fw],
                                in1=A1[:, 512 * bb:512 * bb + fw]), "VE")
                barrier()

    if N_SUB >= 2 and DEV_OUT != "qkv":
        quant_weights()
        emit_ffn()

    def emit_attn_b():
        load_bias_cols(b_qb, 0, 24)
        for (o, wd) in QCH:
            ln_group(lambda c: X[:, c * S + o:c * S + o + wd], wd)
            for c in range(DC):
                normalize_tile(wd, X[:, c * S + o:c * S + o + wd],
                               A1[:, 4096 + 512 * c:4096 + 512 * c + wd])
            barrier()
            gemm_fp32([wsQB[bb] for bb in range(24)],
                      lambda c: A1[:, 4096 + 512 * c:4096 + 512 * c + wd],
                      o, wd,
                      evac_spill(lambda bb: QKB[128 * bb:128 * (bb + 1), :]))
            barrier()
        for h_ in range(H):
            with nc.Block() as blk:
                @blk.sync
                def _(sy, h_=h_):
                    w(sy, "TE")
                    w(sy, "VE")
                    w(sy, "SE")
                    w(sy, "GD")
                    inc(sy.dma_start(out=A2[0:64, 0:2064],
                                     in_=QKB[1024 + 64 * h_:
                                             1024 + 64 * h_ + 64, :]), "LD")
                    inc(sy.dma_start(out=A2[0:64, 2064:4128],
                                     in_=QKB[2048 + 64 * h_:
                                             2048 + 64 * h_ + 64, :]), "LD")
                    inc(sy.dma_start(out=A3[0:64, 0:2064],
                                     in_=QKB[64 * h_:64 * h_ + 64, :]), "LD")

                @blk.vector
                def _(ve):
                    w(ve, "ST")
                    inc(ve.memset(DRW[:, 0:2112], 0.0), "VE")
            barrier()
            for m in range(16):
                km = 128
                nt = 8 if m < 15 else 9
                qw = 16 * nt
                kr = 8 if m < 15 else 16
                qap = bass.AP(A3, 8 * m, [[4224, 64], [129, 16], [1, nt]])
                kap_c = bass.AP(A2, 8 * m, [[4224, 64], [129, 16], [1, 8]])
                vap_c = bass.AP(A2, 2064 + 8 * m,
                                [[4224, 64], [129, 16], [1, 8]])
                has_l = m > 0
                kap_l = bass.AP(A2, 8 * m - 1, [[4224, 64], [129, 8]])
                vap_l = bass.AP(A2, 2064 + 8 * m - 1, [[4224, 64], [129, 8]])
                has_r = True
                if m < 15:
                    tr = 8 * m + 8
                    kap_r = bass.AP(A2, 129 * 8 + tr, [[4224, 64], [129, 8]])
                    vap_r = bass.AP(A2, 2064 + 129 * 8 + tr,
                                    [[4224, 64], [129, 8]])
                else:
                    kap_r = bass.AP(A2, 128, [[4224, 64], [129, 16]])
                    vap_r = bass.AP(A2, 2064 + 128, [[4224, 64], [129, 16]])
                with nc.Block() as blk:
                    @blk.sync
                    def _(sy, m=m, qw=qw, kr=kr):
                        w(sy, "SE")
                        inc(sy.dma_start(out=A3[:, 2064:2064 + qw],
                                         in_=mkc[m][:, 0:qw]), "LD")
                        inc(sy.dma_start(out=A3[0:8, 2208:2208 + qw],
                                         in_=mks[m][0:8, 0:qw]), "LD")
                        inc(sy.dma_start(out=A3[0:16, 2352:2352 + qw],
                                         in_=mks[m][8:24, 0:qw]), "LD")

                    @blk.vector
                    def _(ve, m=m, km=km, kr=kr, has_l=has_l, has_r=has_r,
                          vap_c=vap_c, vap_l=vap_l, vap_r=vap_r):
                        w(ve, "TE")
                        i_ = ve.tensor_copy(out=A3[0:64, 3380:3380 + km],
                                            in_=vap_c)
                        if has_l:
                            i_ = ve.tensor_copy(out=A3[0:64, 3530:3538],
                                                in_=vap_l)
                        if has_r:
                            i_ = ve.tensor_copy(out=A3[0:64, 3540:3540 + kr],
                                                in_=vap_r)
                        inc(i_, "VE")

                    @blk.tensor
                    def _(te, m=m, km=km, kr=kr, has_l=has_l, has_r=has_r):
                        w(te, "VE")
                        i_ = te.transpose(PC[1][0:km, 0:64],
                                          A3[0:64, 3380:3380 + km],
                                          IT[0:64, 0:64])
                        if has_l:
                            i_ = te.transpose(PC[2][0:8, 0:64],
                                              A3[0:64, 3530:3538],
                                              IT[0:64, 0:64])
                        if has_r:
                            i_ = te.transpose(PC[2][0:kr, 64:128],
                                              A3[0:64, 3540:3540 + kr],
                                              IT[0:64, 0:64])
                        inc(i_, "TE")

                    @blk.vector
                    def _(ve, m=m, km=km, kr=kr, has_l=has_l, has_r=has_r):
                        w(ve, "TE")
                        i_ = ve.tensor_copy(out=A3[0:km, 2580:2644],
                                            in_=PC[1][0:km, 0:64])
                        if has_l:
                            i_ = ve.tensor_copy(out=A3[0:8, 2644:2708],
                                                in_=PC[2][0:8, 0:64])
                        if has_r:
                            i_ = ve.tensor_copy(out=A3[0:kr, 2708:2772],
                                                in_=PC[2][0:kr, 64:128])
                        inc(i_, "VE")

                    @blk.vector
                    def _(ve, kap_c=kap_c):
                        w(ve, "TE")
                        inc(ve.tensor_copy(out=A3[0:64, 3680:3808],
                                           in_=kap_c), "VE")

                    @blk.tensor
                    def _(te, m=m, km=km, kr=kr, qw=qw, has_l=has_l,
                          has_r=has_r, kap_l=kap_l, kap_r=kap_r,
                          qap=qap):
                        w(te, "LD")
                        w(te, "VE")
                        i_ = te.matmul(PB0[0:km, 0:qw], A3[0:64, 3680:3808],
                                       qap, start=True, stop=True)
                        if has_l:
                            i_ = te.matmul(PB0[0:8, 512:512 + qw], kap_l,
                                           qap, start=True, stop=True)
                        if has_r:
                            i_ = te.matmul(PB1[0:kr, 0:qw], kap_r, qap,
                                           start=True, stop=True)
                        inc(i_, "TE")

                    @blk.vector
                    def _(ve, m=m, km=km, kr=kr, qw=qw, has_l=has_l,
                          has_r=has_r):
                        w(ve, "TE")
                        i_ = ve.tensor_add(out=PB0[0:km, 0:qw],
                                           in0=PB0[0:km, 0:qw],
                                           in1=A3[0:km, 2064:2064 + qw])
                        if has_l:
                            i_ = ve.tensor_add(out=PB0[0:8, 512:512 + qw],
                                               in0=PB0[0:8, 512:512 + qw],
                                               in1=A3[0:8, 2208:2208 + qw])
                        if has_r:
                            i_ = ve.tensor_add(out=PB1[0:kr, 0:qw],
                                               in0=PB1[0:kr, 0:qw],
                                               in1=A3[0:kr, 2352:2352 + qw])
                        inc(i_, "VE")

                    @blk.scalar
                    def _(se, m=m, km=km, kr=kr, qw=qw, has_l=has_l,
                          has_r=has_r):
                        w(se, "VE")
                        i_ = se.activation(out=A3[0:km, 2780:2780 + qw],
                                           in_=PB0[0:km, 0:qw], func=AF.Exp,
                                           bias=CST[0:km, 1:2],
                                           scale=CST[0:km, 0:1])
                        if has_l:
                            i_ = se.activation(out=A3[0:8, 2924:2924 + qw],
                                               in_=PB0[0:8, 512:512 + qw],
                                               func=AF.Exp,
                                               bias=CST[0:8, 1:2],
                                               scale=CST[0:8, 0:1])
                        if has_r:
                            i_ = se.activation(out=A3[0:kr, 3068:3068 + qw],
                                               in_=PB1[0:kr, 0:qw],
                                               func=AF.Exp,
                                               bias=CST[0:kr, 1:2],
                                               scale=CST[0:kr, 0:1])
                        inc(i_, "SE")

                    @blk.tensor
                    def _(te, m=m, km=km, kr=kr, qw=qw, has_l=has_l,
                          has_r=has_r):
                        w(te, "SE")
                        nside = (1 if has_l else 0) + (1 if has_r else 0)
                        te.matmul(PC[3][0:64, 0:qw], A3[0:km, 2580:2644],
                                  A3[0:km, 2780:2780 + qw],
                                  start=True, stop=(nside == 0))
                        i_ = te.matmul(PB1[0:1, 512:512 + qw],
                                       ON2[0:km, 128:129],
                                       A3[0:km, 2780:2780 + qw],
                                       start=True, stop=(nside == 0))
                        if has_l:
                            te.matmul(PC[3][0:64, 0:qw],
                                      A3[0:8, 2644:2708],
                                      A3[0:8, 2924:2924 + qw],
                                      start=False, stop=(not has_r))
                            i_ = te.matmul(PB1[0:1, 512:512 + qw],
                                           ON2[0:8, 128:129],
                                           A3[0:8, 2924:2924 + qw],
                                           start=False, stop=(not has_r))
                        if has_r:
                            te.matmul(PC[3][0:64, 0:qw],
                                      A3[0:kr, 2708:2772],
                                      A3[0:kr, 3068:3068 + qw],
                                      start=False, stop=True)
                            i_ = te.matmul(PB1[0:1, 512:512 + qw],
                                           ON2[0:kr, 128:129],
                                           A3[0:kr, 3068:3068 + qw],
                                           start=False, stop=True)
                        inc(i_, "TE")

                    @blk.vector
                    def _(ve, m=m, qw=qw, nt=nt):
                        w(ve, "TE")
                        inc(ve.tensor_copy(out=A3[0:64, 3200:3200 + qw],
                                           in_=PC[3][0:64, 0:qw]), "VE")
                        dap = bass.AP(DRW, 8 * m, [[2112, 1], [129, 16],
                                                   [1, nt]])
                        w(ve, "VE")
                        inc(ve.tensor_add(out=dap, in0=dap,
                                          in1=PB1[0:1, 512:512 + qw]), "VE")

                    @blk.gpsimd
                    def _(ge, h_=h_, m=m, qw=qw, nt=nt):
                        w(ge, "VE")
                        dst = bass.AP(CTXB.tensor, (64 * h_) * S + 8 * m,
                                      [[S, 64], [129, 16], [1, nt]])
                        inc(ge.dma_start(out=dst,
                                         in_=A3[0:64, 3200:3200 + qw]), "GD")
            with nc.Block() as blk:
                @blk.scalar
                def _(se, h_=h_):
                    w(se, "VE")
                    inc(se.dma_start(out=DENS[h_:h_ + 1, :],
                                     in_=DRW[:, 0:2064]), "ST")
            barrier()
        load_bias_cols(b_ob, 0, 8)
        for (o, wd) in QCH:
            with nc.Block() as blk:
                @blk.sync
                def _(sy, o=o, wd=wd):
                    w(sy, "TE")
                    w(sy, "VE")
                    for c in range(DC):
                        inc(sy.dma_start(
                            out=A1[:, 4096 + 512 * c:4096 + 512 * c + wd],
                            in_=CTXB[128 * c:128 * (c + 1), o:o + wd]), "LD")
            barrier()
            for c in range(DC):
                with nc.Block() as blk:
                    @blk.sync
                    def _(sy, c=c, o=o, wd=wd):
                        w(sy, "VE")
                        inc(sy.dma_start(out=DRW[:, 0:wd],
                                         in_=DENS[2 * c:2 * c + 1, o:o + wd]),
                            "LD")
                        inc(sy.dma_start(out=DRW[:, 1056:1056 + wd],
                                         in_=DENS[2 * c + 1:2 * c + 2,
                                                  o:o + wd]), "LD")

                    @blk.vector
                    def _(ve, wd=wd):
                        w(ve, "LD")
                        inc(ve.reciprocal(out=DRW[:, 0:wd],
                                          in_=DRW[:, 0:wd]), "VE")
                        inc(ve.reciprocal(out=DRW[:, 1056:1056 + wd],
                                          in_=DRW[:, 1056:1056 + wd]), "VE")

                    @blk.tensor
                    def _(te, wd=wd):
                        w(te, "VE")
                        te.matmul(PB1[0:64, 0:wd], ON2[0:1, 0:64],
                                  DRW[:, 0:wd], start=True, stop=True)
                        inc(te.matmul(PB1[64:128, 0:wd], ON2[0:1, 0:64],
                                      DRW[:, 1056:1056 + wd],
                                      start=True, stop=True,
                                      tile_position=(0, 64)), "TE")

                    @blk.vector
                    def _(ve, c=c, wd=wd):
                        w(ve, "TE")
                        inc(ve.tensor_mul(
                            out=A1[0:64, 4096 + 512 * c:4096 + 512 * c + wd],
                            in0=A1[0:64, 4096 + 512 * c:4096 + 512 * c + wd],
                            in1=PB1[0:64, 0:wd]), "VE")
                        inc(ve.tensor_mul(
                            out=A1[64:128,
                                   4096 + 512 * c:4096 + 512 * c + wd],
                            in0=A1[64:128,
                                   4096 + 512 * c:4096 + 512 * c + wd],
                            in1=PB1[64:128, 0:wd]), "VE")
                barrier()
            gemm_fp32([wsOB[bb] for bb in range(8)],
                      lambda c: A1[:, 4096 + 512 * c:4096 + 512 * c + wd],
                      o, wd, evac_residual())
            barrier()

    if N_SUB >= 3 and DEV_OUT != "qkv":
        emit_attn_b()

    if N_SUB >= 4 and DEV_OUT != "qkv":
        emit_ffn()

    if DEV_OUT == "qkv":
        with nc.Block() as blk:
            @blk.sync
            def _(sy):
                inc(sy.dma_start(out=out[:, :], in_=QS[:, :]), "GD")
                inc(sy.dma_start(out=out2[0:D, :], in_=KS[:, :]), "GD")
                inc(sy.dma_start(out=out2[D:2 * D, :], in_=VS[:, :]), "GD")
        barrier()
        _finish(nc, b, out, out2, X, emit_x=False, junk_rows=(2 * D, 3 * D))
        return nc

    _finish(nc, b, out, out2, X, emit_x=True)
    return nc



def _finish(nc, b, out, out2, X, emit_x=True, junk_rows=(0, 3 * D)):
    inc, w = b.inc, b.w
    with nc.Block() as blk:
        @blk.scalar
        def _(se):
            for s_ in ("TE", "VE", "GE", "LD", "GD"):
                w(se, s_)
            if emit_x:
                for c in range(DC):
                    inc(se.dma_start(out=out[128 * c:128 * (c + 1), :],
                                     in_=X[:, c * S:(c + 1) * S]), "ST")
            if out2 is not None:
                for r0 in range(junk_rows[0], junk_rows[1], 128):
                    inc(se.dma_start(out=out2[r0:r0 + 128, 0:S],
                                     in_=X[:, 0:S]), "ST")
                    inc(se.dma_start(out=out2[r0:r0 + 128, S:SF],
                                     in_=X[:, 0:SF - S]), "ST")

        @blk.gpsimd
        def _(ge):
            w(ge, "ST")
            w(ge, "GD")
            w(ge, "SE")


def _stage_w(wmat, nblk):
    Din, Dout = wmat.shape
    nch = Din // 128
    t = wmat.reshape(nch, 128, nblk, 128).transpose(2, 1, 0, 3)
    return np.ascontiguousarray(t.reshape(nblk, 128, nch * 128))


def kernel(L0_attn_a_x, L0_attn_a_W_in, L0_attn_a_b_in, L0_attn_a_W_out,
           L0_attn_a_b_out, L0_attn_a_rel_emb, L0_attn_b_token_ids,
           L0_attn_b_w_qkv, L0_attn_b_b_qkv, L0_attn_b_w_out, L0_attn_b_b_out,
           L0_ffn_w1, L0_ffn_b1, L0_ffn_w2, L0_ffn_b2):
    x = np.asarray(L0_attn_a_x, dtype=np.float32)

    shared = {
        "wsA": _stage_w(np.asarray(L0_attn_a_W_in, np.float32), 24),
        "b_in": np.asarray(L0_attn_a_b_in, np.float32).reshape(3072, 1),
        "wsOA": _stage_w(np.asarray(L0_attn_a_W_out, np.float32), 8),
        "b_oa": np.asarray(L0_attn_a_b_out, np.float32).reshape(D, 1),
        "w1t": np.ascontiguousarray(np.asarray(L0_ffn_w1, np.float32).T),
        "b1": np.asarray(L0_ffn_b1, np.float32).reshape(4096, 1),
        "w2t": np.ascontiguousarray(np.asarray(L0_ffn_w2, np.float32).T),
        "b2": np.asarray(L0_ffn_b2, np.float32).reshape(D, 1),
        "wsQB": _stage_w(np.ascontiguousarray(
            np.asarray(L0_attn_b_w_qkv, np.float32).T), 24),
        "b_qb": np.asarray(L0_attn_b_b_qkv, np.float32).reshape(3072, 1),
        "wsOB": _stage_w(np.ascontiguousarray(
            np.asarray(L0_attn_b_w_out, np.float32).T), 8),
        "b_ob": np.asarray(L0_attn_b_b_out, np.float32).reshape(D, 1),
        "idin": _mk_idin(),
    }

    mkc = np.full((16, 128, 144), np.float32(-1e30), np.float32)
    mks = np.full((16, 24, 144), np.float32(-1e30), np.float32)
    for m in range(16):
        nt = 8 if m < 15 else 9
        qw = 16 * nt
        qn = [16 * (8 * m + j // 16) + (15 - (j % 16)) for j in range(qw)]
        qn = [16 * (8 * m + (j % nt)) + (15 - (j // nt)) for j in range(qw)]
        for p in range(128):
            rd, t2 = p // 8, p % 8
            k = 16 * (8 * m + t2) + (15 - rd)
            for j in range(qw):
                q = qn[j]
                if q < S and abs(k - q) <= 8:
                    mkc[m, p, j] = 0.0
        if m > 0:
            for p in range(8):
                k = 16 * (8 * m - 1) + (15 - p)
                for j in range(qw):
                    q = qn[j]
                    if q < S and abs(k - q) <= 8:
                        mks[m, p, j] = 0.0
        kr = 8 if m < 15 else 16
        for i in range(kr):
            if m < 15:
                k = 16 * (8 * m + 8) + (15 - (8 + i))
            else:
                k = 16 * 128 + (15 - i)
            for j in range(qw):
                q = qn[j]
                if q < S and abs(k - q) <= 8:
                    mks[m, 8 + i, j] = 0.0
    shared["mkc"] = mkc
    shared["mks"] = mks

    rel = np.asarray(L0_attn_a_rel_emb, np.float32)
    T = np.concatenate([rel[:4095], np.repeat(rel[4095:4096], 8192 - 4095, 0)],
                       axis=0)
    tabs = {}
    for A in (0, 2032):
        xcols = np.arange(8072)
        idx = np.clip(8191 - xcols - A // 16, 0, 8191)
        tab = T[idx, :].T
        tab2 = np.zeros((128, 8072), np.float32)
        for u in range(8):
            sh = 7 - u
            tab2[16 * u:16 * (u + 1), :8072 - sh] = tab[:, sh:]
        tabs[A] = np.ascontiguousarray(tab2)

    nc = build_program()

    perm = _sperm()
    in_maps = []
    for core in range(8):
        seq, half = core // 2, core % 2
        A = 0 if half == 0 else 2032
        xs = np.ascontiguousarray(x[seq].T)
        d = dict(shared)
        d["xe"] = np.ascontiguousarray(xs[:, A + perm])
        d["xf"] = xs
        d["btab"] = tabs[A]
        in_maps.append(d)

    res = run_bass_kernel_spmd(nc, in_maps, core_ids=list(range(8)))
    if DEV_OUT is not None:
        return res, in_maps

    y = np.zeros((4, SF, D), np.float32)
    for core in range(8):
        seq, half = core // 2, core % 2
        A = 0 if half == 0 else 2032
        o = res.results[core]["out"]
        own = (perm >= 16) if half == 1 else (perm < 2048)
        qn = A + perm[own]
        y[seq, qn, :] = o[:, own].T
    return y


def _mk_idin():
    idin = np.zeros((128, 256), np.float32)
    i64 = np.eye(64, dtype=np.float32)
    idin[0:64, 0:64] = i64
    idin[64:128, 0:64] = i64
    idin[:, 64:192] = 8 * np.eye(128, dtype=np.float32)
    return idin


